# version 01 (final)
# baseline (speedup 1.0000x reference)
import contextlib
import numpy as np
import concourse.bass as bass
import concourse.mybir as mybir
from concourse.bass_utils import run_bass_kernel_spmd
from concourse.bass import IndirectOffsetOnAxis

F32 = mybir.dt.float32
BF16 = mybir.dt.bfloat16
I32 = mybir.dt.int32
ALU = mybir.AluOpType
ACT = mybir.ActivationFunctionType
AX = mybir.AxisListType.X
RG = [[0, 1, 2, 3, 4, 5, 6, 7]]

B, L, D, DI, DS, DC = 2, 8192, 1024, 2048, 16, 4
E, FF, CAP = 8, 4096, 2048
TOK = 2048
NT4 = 4
NT16 = 16
XC = TOK + 4
XW = 516
PROW = 2056
PW = 1032
YROW = 2049
EPS = 1e-5
NCH = DI // 128
NDC = D // 128

DEBUG = False
RACECHECK = False
CUT = 0
RTILES = 16
MQ = 0
VSTEP = 2


class S:
    def __init__(self, h):
        self.h = h
        self.n = 0

    def inc(self, instr, k=1):
        instr.then_inc(self.h, k)
        self.n += k

    def bump(self, eng, k=1):
        eng.sem_inc(self.h, k)
        self.n += k


def build_nc():
    nc = bass.Bass("TRN2", target_bir_lowering=False, debug=False,
                   detect_race_conditions=RACECHECK)
    ap = lambda t: t.ap()

    x_tm_d = ap(nc.dram_tensor("x_tm", [TOK, D], F32, kind="ExternalInput"))
    x_fm_d = ap(nc.dram_tensor("x_fm", [D, XC], F32, kind="ExternalInput"))
    w_in_d = ap(nc.dram_tensor("w_in", [D, 2 * DI], F32, kind="ExternalInput"))
    convw_d = ap(nc.dram_tensor("convw", [128, NCH * DC], F32, kind="ExternalInput"))
    convb_d = ap(nc.dram_tensor("convb", [128, NCH], F32, kind="ExternalInput"))
    wdel_d = ap(nc.dram_tensor("wdel", [128, NCH * DS], F32, kind="ExternalInput"))
    bdel_d = ap(nc.dram_tensor("bdel", [DS, 1], F32, kind="ExternalInput"))
    wB_d = ap(nc.dram_tensor("wB", [128, NCH * DS], F32, kind="ExternalInput"))
    wC_d = ap(nc.dram_tensor("wC", [128, NCH * DS], F32, kind="ExternalInput"))
    dvec_d = ap(nc.dram_tensor("dvec", [128, NCH], F32, kind="ExternalInput"))
    wout_d = ap(nc.dram_tensor("wout", [128, NDC], F32, kind="ExternalInput"))
    woutbc_d = ap(nc.dram_tensor("woutbc", [128, D], F32, kind="ExternalInput"))
    rw_d = ap(nc.dram_tensor("rw", [128, NDC * E], F32, kind="ExternalInput"))
    w1_d = ap(nc.dram_tensor("w1", [D, FF], F32, kind="ExternalInput"))
    w2_d = ap(nc.dram_tensor("w2", [FF, D], F32, kind="ExternalInput"))
    idbf_d = ap(nc.dram_tensor("idbf", [128, 128], BF16, kind="ExternalInput"))
    idf32_d = ap(nc.dram_tensor("idf32", [128, 128], F32, kind="ExternalInput"))
    loga_d = ap(nc.dram_tensor("loga", [DS, 1], F32, kind="ExternalInput"))
    ones128_d = ap(nc.dram_tensor("ones128", [128, 1], F32, kind="ExternalInput"))
    onesrow_d = ap(nc.dram_tensor("onesrow", [1, 128], F32, kind="ExternalInput"))
    iota8bc_d = ap(nc.dram_tensor("iota8bc", [128, E], F32, kind="ExternalInput"))
    iotas_d = ap(nc.dram_tensor("iotas", [128, NT16], F32, kind="ExternalInput"))
    foldsel_d = ap(nc.dram_tensor("foldsel", [E, 4], F32, kind="ExternalInput"))
    fixa_d = ap(nc.dram_tensor("fixa", [DS, 4], F32, kind="ExternalInput"))
    prevmask_d = ap(nc.dram_tensor("prevmask", [E, 1], F32, kind="ExternalInput"))
    ltmaskebc_d = ap(nc.dram_tensor("ltmaskebc", [E, E], F32, kind="ExternalInput"))
    onehotebc_d = ap(nc.dram_tensor("onehotebc", [E, E], F32, kind="ExternalInput"))
    trilst_d = ap(nc.dram_tensor("trilst", [E, E], F32, kind="ExternalInput"))
    j2056_d = ap(nc.dram_tensor("j2056", [E, 1], F32, kind="ExternalInput"))
    diffm_d = ap(nc.dram_tensor("diffm", [E, E], F32, kind="ExternalInput"))

    out_d = ap(nc.dram_tensor("out", [TOK, D], F32, kind="ExternalOutput"))
    if DEBUG:
        dbgx1_d = ap(nc.dram_tensor("dbgx1", [TOK, D], F32, kind="ExternalOutput"))
        dbgyk_d = ap(nc.dram_tensor("dbgyk", [YROW, D], F32, kind="ExternalOutput"))
        dbgcol_d = ap(nc.dram_tensor("dbgcol", [128, 10 * NT16], F32, kind="ExternalOutput"))
        dbgfm_d = ap(nc.dram_tensor("dbgfm", [48, TOK], F32, kind="ExternalOutput"))

    ag0i_d = ap(nc.dram_tensor("ag0i", [1, 64], F32, kind="Internal"))
    ag0o_d = ap(nc.dram_tensor("ag0o", [E, 64], F32, kind="Internal"))
    pk_d = ap(nc.dram_tensor("pk", [PROW, PW], F32, kind="Internal"))
    pall_d = ap(nc.dram_tensor("pall", [E * PROW, PW], F32, kind="Internal"))
    yk_d = ap(nc.dram_tensor("yk", [YROW, D], F32, kind="Internal"))
    yall_d = ap(nc.dram_tensor("yall", [E * YROW, D], F32, kind="Internal"))
    x1_d = ap(nc.dram_tensor("x1d", [TOK, D], F32, kind="Internal"))

    es = contextlib.ExitStack()
    with es:
        _uid = [0]

        def _nm(p):
            _uid[0] += 1
            return "%s%03d" % (p, _uid[0])

        sb = lambda shape, dt=F32, **kw: es.enter_context(nc.sbuf_tensor(_nm("sb"), shape, dt, **kw))
        block = es.enter_context(nc.Block())

        MANUAL = []

        def scoped_sb(shape, dt=F32):
            c = nc.sbuf_tensor(_nm("sc"), shape, dt)
            MANUAL.append(c)
            return c, c.__enter__()

        def close_manual():
            while MANUAL:
                MANUAL.pop().__exit__(None, None, None)

        def scoped_ps(shape, dt=F32):
            c = nc.psum_tensor(_nm("pp"), shape, dt)
            return c, c.__enter__()

        def sem():
            return S(es.enter_context(nc.semaphore(_nm("sm"))))

        idbf = sb([128, 128], BF16)
        idf32 = sb([128, 128], F32)
        loga = sb([DS, 1]); ones128 = sb([128, 1]); onesrow = sb([1, 128])
        iota8bc = sb([128, E]); iotas = sb([128, NT16])
        foldsel = sb([E, 4]); fixa = sb([DS, 4]); prevmask = sb([E, 1])
        ltmaskebc = sb([E, E]); onehotebc = sb([E, E]); trilst = sb([E, E])
        j2056 = sb([E, 1]); diffm = sb([E, E])
        woutbc = sb([128, D])
        ones16 = sb([DS, 1]); bdel = sb([DS, 1])

        cw = sb([128, NCH, DC]); cb = sb([128, NCH, 1])
        wdel = sb([128, NCH, DS]); wBs = sb([128, NCH, DS]); wCs = sb([128, NCH, DS])
        dv = sb([128, NCH, 1]); wofm = sb([128, NDC, 1])
        rwf = sb([128, NDC, E])

        maskstm = sb([128, NT16, E])
        mskbf = sb([128, NT16, E], BF16)
        ycols = sb([128, NT16]); sumc = sb([128, NT16]); sumsqc = sb([128, NT16])
        mcols = sb([128, NT16]); rstdc = sb([128, NT16]); nmrstd = sb([128, NT16])
        negm = sb([128, NT16])
        gatec = sb([128, NT16]); ecols = sb([128, NT16]); rlocc = sb([128, NT16])
        gposc = sb([128, NT16]); gposi = sb([128, NT16], I32)
        grankc = sb([128, NT16]); yidxc = sb([128, NT16]); yidxi = sb([128, NT16], I32)
        slotm = sb([128, NT16]); gateE = sb([128, NT16])
        srcc = sb([128, NT16]); srci = sb([128, NT16], I32); t16 = sb([128, NT16])

        wor_bc = sb([128, E]); srw_bc = sb([128, E])
        gprefT_bc = sb([128, E]); offsT_bc = sb([128, E])
        coffb = sb([128, E]); cprb = sb([128, E]); countEb = sb([128, 1])
        dconst = sb([DS, 1]); dloga = sb([DS, 1]); bconst = sb([DS, 1]); cconst = sb([DS, 1])
        dcD = sb([1, 1])
        a4 = sb([DS, 4]); h4 = sb([DS, 4])
        hf = [sb([DS, 1]) for _ in range(5)]
        hinit = sb([DS, 1])
        ag0st = sb([1, 64]); ag0sb = sb([E, 64])
        countsl = sb([E, 1]); gpref = sb([E, 1]); crow = sb([1, E])
        wor_row = sb([1, E]); srw_row = sb([1, E]); gprow = sb([1, E])
        ctab = sb([E, E]); countsE = sb([E, 1]); coff = sb([E, 1]); goffE = sb([E, 1])
        cvals = sb([E, 1]); cpr = sb([E, 1])
        zrow = sb([1, D]); zpad = sb([E, PW])
        sumdel = sb([DS, 1]); aprod = sb([DS, 1]); hend0 = sb([DS, 1])
        t8a = sb([128, E]); t8b = sb([128, E]); lgt = sb([128, E])
        lmax = sb([128, 1]); nlmax = sb([128, 1]); sexp = sb([128, 1]); expl = sb([128, E])
        tv1 = sb([128, 1]); tv2 = sb([128, 1])
        xa = [sb([128, PW]) for _ in range(2)]
        sqs = sb([128, D])

        s_c0 = sem(); s_g0 = sem(); s_wb = sem()
        s_pz = sem(); s_sil = sem(); s_xg = sem()
        s_pp = sem(); s_pje = sem(); s_bde = sem()
        s_sc = sem(); s_ag0 = sem(); s_fold = sem(); s_hi = sem(); s_h = sem()
        s_yps = sem(); s_ye = sem(); s_yt = sem(); s_yc = sem()
        s_xt = sem(); s_mx = sem(); s_sq = sem(); s_x1s = sem()
        s_xf = sem(); s_xr = sem(); s_xre = sem(); s_xfl = sem()
        s_xrt = sem(); s_lg = sem(); s_ex = sem(); s_mk = sem(); s_mke = sem()
        s_cn = sem(); s_ct2 = sem(); s_rk = sem()
        s_pr = sem(); s_pks = sem()
        s_ag1 = sem(); s_idm = sem(); s_ide = sem()
        s_gt = sem(); s_gf = sem(); s_tr = sem(); s_te = sem()
        s_m1 = sem(); s_ge = sem(); s_m2 = sem(); s_oe = sem(); s_ys = sem()
        s_ag2 = sem(); s_xl = sem(); s_ac = sem(); s_l2 = sem(); s_q2 = sem(); s_o = sem()
        s_wl = sem(); s_dbg = sem(); s_meta = sem()
        s_cc0 = sem(); s_cc1 = sem(); s_cc2 = sem()
        s_xfe = sem(); s_xfo = sem()
        s_xtp = [sem(), sem()]; s_xlp = [sem(), sem()]; s_x3p = [sem(), sem()]
        s_gtp = [sem(), sem()]; s_acp = [sem(), sem()]; s_ysp = [sem(), sem()]
        s_pkp = [sem(), sem()]

        G0L = [0]

        @block.sync
        def _(sy):
            for dst, src in [
                (idf32, idf32_d), (loga, loga_d), (ones128, ones128_d),
                (onesrow, onesrow_d), (iota8bc, iota8bc_d), (iotas, iotas_d),
                (foldsel, foldsel_d), (fixa, fixa_d), (prevmask, prevmask_d),
                (ltmaskebc, ltmaskebc_d), (onehotebc, onehotebc_d),
                (trilst, trilst_d), (j2056, j2056_d), (diffm, diffm_d),
                (woutbc, woutbc_d), (idbf, idbf_d), (bdel, bdel_d),
            ]:
                s_c0.inc(sy.dma_start(out=dst[:], in_=src[:]), 16)

        winbf_c, winbf = scoped_sb([128, NDC, 2 * DI], BF16)
        xfmt_c = [nc.sbuf_tensor(_nm("xfmt"), [128, NDC, 512], BF16) for _ in range(2)]
        MANUAL.append(xfmt_c[0]); MANUAL.append(xfmt_c[1])
        xfmt = [c.__enter__() for c in xfmt_c]
        xfmh_c, xfmh = scoped_sb([128, NDC, 4], BF16)
        xgw_c, xgw = scoped_sb([128, NCH, XW], BF16)
        wallb_c, wallb = scoped_sb([128, DC, NCH, 112], BF16)
        xfc_c, xfc = scoped_sb([128, XC], F32)
        strip_c, strip = scoped_sb([DS, 5, TOK], F32)
        rows_c, yrow = scoped_sb([1, TOK], F32)
        sil_c = [nc.sbuf_tensor(_nm("sil"), [128, 512], F32) for _ in range(2)]
        MANUAL.extend(sil_c)
        sil = [c.__enter__() for c in sil_c]
        bmt_c, bmt = scoped_sb([DS, 512], F32)

        delta = strip[:, 0, :]
        a_d = strip[:, 1, :]
        b_d = strip[:, 2, :]
        cmx = strip[:, 3, :]
        hbuf = strip[:, 4, :]
        cmh = strip[:, 1, :]
        xrw = strip[0:E, 0, :]
        masksfm = strip[0:E, 2, :]
        cntfm = strip[0:E, 4, :]
        onesfm = strip[0:E, 3, :]

        @block.gpsimd
        def _(g):
            for dst, srcd in [
                (cw, convw_d), (cb, convb_d), (wdel, wdel_d), (wBs, wB_d),
                (wCs, wC_d), (dv, dvec_d), (wofm, wout_d), (rwf, rw_d),
            ]:
                s_g0.inc(g.dma_start(out=dst[:], in_=srcd[:]), 16)
            s_g0.inc(g.dma_start(out=winbf[:], in_=w_in_d.rearrange("(c p) f -> p c f", p=128)), 16)
            G0L[0] = s_g0.n
            g.wait_ge(s_meta.h, 1)
            s_g0.inc(g.dma_start(out=pk_d[CAP:CAP + E, :], in_=zpad[:]), 16)
            s_xfl.inc(g.dma_start(out=xfmh[:],
                                  in_=x_fm_d.rearrange("(c p) t -> p c t", p=128)[:, :, 0:4]), 16)
            for ti in range(NT4):
                if ti >= 2:
                    g.wait_ge(s_pz.h, 16 * ti)
                sx = s_xfe if ti % 2 == 0 else s_xfo
                sx.inc(g.dma_start(
                    out=xfmt[ti % 2][:],
                    in_=x_fm_d.rearrange("(c p) t -> p c t", p=128)[:, :, 4 + 512 * ti:4 + 512 * (ti + 1)]), 16)
        G0_DONE = s_g0.n

        ps_a_c, ps_a = scoped_ps([128, 512], F32)
        ps_b_c, ps_b = scoped_ps([128, 512], F32)

        @block.vector
        def _(v):
            v.wait_ge(s_g0.h, G0L[0])
            v.wait_ge(s_c0.h, s_c0.n)
            v.memset(wallb[:], 0.0)
            v.memset(xa[0][:, D:PW], 0.0)
            v.memset(xa[1][:, D:PW], 0.0)
            li0 = v.memset(zpad[:], 0.0)
            li0.then_inc(s_meta.h, 1)
            v.memset(ones16[:], 1.0)
            v.memset(zrow[:], 0.0)
            v.drain()
            for k in range(DC):
                for c in range(NCH):
                    wk = cw[:, c, k:k + 1]
                    v.tensor_scalar(wallb[:, k, c, 0:DS], wdel[:, c, :], wk, None, op0=ALU.mult)
                    v.tensor_scalar(wallb[:, k, c, 32:32 + DS], wBs[:, c, :], wk, None, op0=ALU.mult)
                    v.tensor_scalar(wallb[:, k, c, 64:64 + DS], wCs[:, c, :], wk, None, op0=ALU.mult)
                    v.tensor_scalar(wallb[:, k, c, 96:97], dv[:, c, :], wk, None, op0=ALU.mult)
            s_wb.inc(v.drain())

        @block.tensor
        def _(t):
            t.wait_ge(s_wb.h, 1)
            for c in range(NCH):
                t.matmul(ps_a[0:DS, 0:1], wdel[:, c, :], cb[:, c, :], start=(c == 0), stop=(c == NCH - 1))
            for c in range(NCH):
                t.matmul(ps_a[0:DS, 1:2], wBs[:, c, :], cb[:, c, :], start=(c == 0), stop=(c == NCH - 1))
            for c in range(NCH):
                t.matmul(ps_a[0:DS, 2:3], wCs[:, c, :], cb[:, c, :], start=(c == 0), stop=(c == NCH - 1))
            for c in range(NCH):
                t.matmul(ps_a[64:65, 0:1], dv[:, c, :], cb[:, c, :], start=(c == 0), stop=(c == NCH - 1))
            for c in range(NDC):
                t.matmul(ps_a[64:65, 8:16], wofm[:, c, :], rwf[:, c, :], start=(c == 0), stop=(c == NDC - 1))
            for c in range(NDC):
                t.matmul(ps_a[64:65, 16:24], ones128[:], rwf[:, c, :], start=(c == 0), stop=(c == NDC - 1))
            s_wb.inc(t.drain())

        @block.vector
        def _(v):
            v.wait_ge(s_wb.h, 2)
            v.tensor_add(dconst[:], ps_a[0:DS, 0:1], bdel[:])
            v.tensor_copy(bconst[:], ps_a[0:DS, 1:2])
            v.tensor_copy(cconst[:], ps_a[0:DS, 2:3])
            v.drain()
            v.tensor_mul(dloga[:], dconst[:], loga[:])
            s_wb.inc(v.drain())

        @block.scalar
        def _(s):
            s.wait_ge(s_wb.h, 3)
            s.copy(dcD[:], ps_a[64:65, 0:1])
            s.copy(wor_row[:], ps_a[64:65, 8:16])
            s.copy(srw_row[:], ps_a[64:65, 16:24])
            s_wb.inc(s.drain())

        @block.tensor
        def _(t):
            t.wait_ge(s_wb.h, 4)
            t.matmul(ps_b[0:128, 0:8], onesrow[:], wor_row[:], start=True, stop=True)
            t.matmul(ps_b[0:128, 8:16], onesrow[:], srw_row[:], start=True, stop=True)
            s_wb.inc(t.drain())

        @block.vector
        def _(v):
            v.wait_ge(s_wb.h, 5)
            v.tensor_copy(wor_bc[:], ps_b[0:128, 0:8])
            v.tensor_copy(srw_bc[:], ps_b[0:128, 8:16])
            s_wb.inc(v.drain())
        WB_DONE = 6

        pz_c = [scoped_ps([128, 512], F32) for _ in range(4)]
        pz = [c[1] for c in pz_c]
        pproj_c, pproj = scoped_ps([112, 512], F32)
        ptm_c, ptm = scoped_ps([128, 256], BF16)

        xz_groups = [(0, 0, 4)] + [(4 + 512 * i, 4, 512) for i in range(NT4)]
        npairs = len(xz_groups) * NCH

        @block.tensor
        def _(t):
            t.wait_ge(s_wb.h, WB_DONE)
            gi = 0
            for gidx, (c0, w0, n) in enumerate(xz_groups):
                ti = gidx - 1
                if ti < 0:
                    t.wait_ge(s_xfl.h, 16)
                elif ti % 2 == 0:
                    t.wait_ge(s_xfe.h, 16 * (ti // 2 + 1))
                else:
                    t.wait_ge(s_xfo.h, 16 * (ti // 2 + 1))
                xsrc = xfmh if ti < 0 else xfmt[ti % 2]
                for c in range(NCH):
                    par = gi % 2
                    if gi >= 2:
                        t.wait_ge(s_xg.h, gi - 1)
                    for d in range(NDC):
                        t.matmul(pz[2 * par][:, 0:n], winbf[:, d, c * 128:(c + 1) * 128],
                                 xsrc[:, d, 0:n], start=(d == 0), stop=(d == NDC - 1))
                    for d in range(NDC):
                        li = t.matmul(pz[2 * par + 1][:, 0:n], winbf[:, d, DI + c * 128:DI + (c + 1) * 128],
                                      xsrc[:, d, 0:n], start=(d == 0), stop=(d == NDC - 1))
                    s_pz.inc(li)
                    gi += 1
                if ti >= 0:
                    t.wait_ge(s_xg.h, 16 * (gidx + 1))
                    if ti > 0:
                        t.wait_ge(s_pje.h, ti)
                        t.wait_ge(s_bde.h, ti)
                    for k in range(DC):
                        for c in range(NCH):
                            li = t.matmul(pproj[:, :], wallb[:, k, c, :],
                                          xgw[:, c, 1 + k: 513 + k],
                                          start=(k == 0 and c == 0), stop=(k == DC - 1 and c == NCH - 1))
                    s_pp.inc(li)

        @block.scalar
        def _(s):
            gi = 0
            for gidx, (c0, w0, n) in enumerate(xz_groups):
                for c in range(NCH):
                    par = gi % 2
                    s.wait_ge(s_pz.h, gi + 1)
                    if gi >= 2:
                        s.wait_ge(s_xg.h, gi - 1)
                    s_sil.inc(s.activation(sil[par][:, 0:n], pz[2 * par][:, 0:n], ACT.Silu))
                    gi += 1
                ti = gidx - 1
                if ti >= 0:
                    s.wait_ge(s_pp.h, ti + 1)
                    c0p = 512 * ti
                    s.activation(delta[:, c0p:c0p + 512], pproj[0:DS, :], ACT.Identity, bias=dconst[:])
                    s.activation(a_d[:, c0p:c0p + 512], pproj[0:DS, :], ACT.Exp, bias=dloga[:], scale=loga[:])
                    s.activation(cmx[:, c0p:c0p + 512], pproj[64:64 + DS, :], ACT.Identity, bias=cconst[:])
                    s_pje.inc(s.activation(yrow[0:1, c0p:c0p + 512], pproj[96:97, :], ACT.Identity, bias=dcD[:]))

        @block.vector
        def _(v):
            gi = 0
            for gidx, (c0, w0, n) in enumerate(xz_groups):
                ti = gidx - 1
                if ti >= 1:
                    v.wait_ge(s_pje.h, ti)
                    c0p = 512 * (ti - 1)
                    v.tensor_scalar(bmt[:, :], pproj[32:32 + DS, :], bconst[:], None, op0=ALU.add)
                    v.drain()
                    s_bde.inc(v.tensor_mul(b_d[:, c0p:c0p + 512], delta[:, c0p:c0p + 512], bmt[:, :]))
                    for c in range(NCH):
                        v.tensor_copy(xgw[:, c, 0:4], xgw[:, c, 512:516])
                    v.drain()
                for c in range(NCH):
                    par = gi % 2
                    v.wait_ge(s_sil.h, gi + 1)
                    s_xg.inc(v.tensor_mul(xgw[:, c, w0:w0 + n], sil[par][:, 0:n], pz[2 * par + 1][:, 0:n]))
                    gi += 1
            v.wait_ge(s_pje.h, NT4)
            v.tensor_scalar(bmt[:, :], pproj[32:32 + DS, :], bconst[:], None, op0=ALU.add)
            v.drain()
            s_bde.inc(v.tensor_mul(b_d[:, 512 * (NT4 - 1):512 * NT4],
                                   delta[:, 512 * (NT4 - 1):512 * NT4], bmt[:, :]))
        XG_DONE = npairs



        @block.vector
        def _(v):
            v.wait_ge(s_bde.h, NT4)
            v.tensor_tensor_scan(hbuf[:, :], a_d[:, :], b_d[:, :], initial=0.0,
                                 op0=ALU.mult, op1=ALU.add)
            v.drain()
            v.tensor_copy(hend0[:], hbuf[:, TOK - 1:TOK])
            v.reduce_sum(sumdel[:], delta[:, :], axis=AX)
            s_sc.inc(v.drain())

        @block.scalar
        def _(s):
            s.wait_ge(s_sc.h, 1)
            s.activation(aprod[:], sumdel[:], ACT.Exp, scale=loga[:])
            s_sc.inc(s.drain())

        @block.tensor
        def _(t):
            t.wait_ge(s_sc.h, 2)
            t.transpose(ps_a[0:1, 24:24 + DS], hend0[:], idf32[0:DS, 0:DS])
            t.transpose(ps_a[0:1, 40:40 + DS], aprod[:], idf32[0:DS, 0:DS])
            s_sc.inc(t.drain())

        @block.vector
        def _(v):
            v.wait_ge(s_sc.h, 3)
            v.memset(ag0st[:], 0.0)
            v.drain()
            v.tensor_copy(ag0st[0:1, 0:DS], ps_a[0:1, 24:24 + DS])
            v.tensor_copy(ag0st[0:1, 32:32 + DS], ps_a[0:1, 40:40 + DS])
            s_sc.inc(v.drain())

        @block.gpsimd
        def _(g):
            g.wait_ge(s_sc.h, 4)
            s_ag0.inc(g.dma_start(out=ag0i_d[:], in_=ag0st[:]), 16)
            g.wait_ge(s_ag0.h, 16)
            s_cc0.inc(g.collective_compute(
                "AllGather", ALU.bypass, ins=[ag0i_d.opt()], outs=[ag0o_d.opt()],
                replica_groups=RG), 1)
            g.wait_ge(s_cc0.h, 1)
            s_ag0.inc(g.dma_start(out=ag0sb[:], in_=ag0o_d[:]), 16)
        AG0_DONE = 32

        @block.tensor
        def _(t):
            t.wait_ge(s_ag0.h, AG0_DONE)
            t.matmul(ps_b[0:DS, 0:4], ag0sb[:, 0:DS], foldsel[:], start=True, stop=True)
            t.matmul(ps_b[32:32 + DS, 0:4], ag0sb[:, 32:32 + DS], foldsel[:], start=True, stop=True)
            s_fold.inc(t.drain())

        @block.vector
        def _(v):
            v.wait_ge(s_fold.h, 1)
            v.tensor_copy(h4[:, :], ps_b[0:DS, 0:4])
            v.tensor_add(a4[:, :], ps_b[32:32 + DS, 0:4], fixa[:])
            v.memset(hf[0][:], 0.0)
            v.drain()
            for c in range(4):
                v.scalar_tensor_tensor(hf[c + 1][:], a4[:, c:c + 1], hf[c][:], h4[:, c:c + 1],
                                       op0=ALU.mult, op1=ALU.add)
                v.drain()
            s_fold.inc(v.drain())

        @block.scalar
        def _(s):
            s.wait_ge(s_fold.h, 2)
            s.copy(hinit[:], hf[4][:])
            s_hi.inc(s.drain())

        @block.vector
        def _(v):
            v.wait_ge(s_hi.h, 1)
            v.tensor_tensor_scan(hbuf[:, :], a_d[:, :], b_d[:, :], initial=hinit[:],
                                 op0=ALU.mult, op1=ALU.add)
            v.drain()
            v.tensor_mul(cmh[:, :], cmx[:, :], hbuf[:, :])
            s_h.inc(v.drain())

        @block.tensor
        def _(t):
            t.wait_ge(s_h.h, 1)
            for q in range(4):
                if q > 0:
                    t.wait_ge(s_ye.h, q)
                s_yps.inc(t.matmul(ps_a[0:1, 0:512], ones16[:], cmh[:, 512 * q:512 * q + 512],
                                   start=True, stop=True))

        @block.vector
        def _(v):
            for q in range(4):
                v.wait_ge(s_yps.h, q + 1)
                s_ye.inc(v.tensor_add(yrow[0:1, 512 * q:512 * q + 512], ps_a[0:1, 0:512],
                                      yrow[0:1, 512 * q:512 * q + 512]))

        @block.tensor
        def _(t):
            t.wait_ge(s_ye.h, 4)
            for c in range(NT16):
                li = t.transpose(ps_b[0:128, c:c + 1], yrow[0:1, 128 * c:128 * (c + 1)], idf32[0:1, 0:1])
            s_yt.inc(li)

        @block.vector
        def _(v):
            v.wait_ge(s_yt.h, 1)
            s_yc.inc(v.tensor_copy(ycols[:], ps_b[0:128, 0:NT16]))

        @block.sync
        def _(sy):
            for c in range(2):
                s_xtp[c].inc(sy.dma_start(out=xa[c][:, 0:D],
                                          in_=x_tm_d.rearrange("(c p) q -> p c q", p=128)[:, c, :]), 16)
            for c in range(NT16):
                par = c % 2
                sy.wait_ge(s_mx.h, 3 * c + 3)
                s_x1s.inc(sy.dma_start(out=x1_d.rearrange("(c p) q -> p c q", p=128)[:, c, :],
                                       in_=xa[par][:, 0:D]), 16)
                if c + 2 < NT16:
                    sy.wait_ge(s_x1s.h, 16 * (c + 1))
                    s_xtp[par].inc(sy.dma_start(out=xa[par][:, 0:D],
                                                in_=x_tm_d.rearrange("(c p) q -> p c q", p=128)[:, c + 2, :]), 16)
        X1S_DONE = 16 * NT16

        @block.vector
        def _(v):
            v.wait_ge(s_yc.h, 1)
            for c in range(NT16):
                par = c % 2
                v.wait_ge(s_xtp[par].h, 16 * (c // 2 + 1))
                v.scalar_tensor_tensor(xa[par][:, 0:D], woutbc[:, :], ycols[:, c:c + 1],
                                       xa[par][:, 0:D], op0=ALU.mult, op1=ALU.add,
                                       accum_out=sumc[:, c:c + 1])
                s_mx.inc(v.drain())
                v.wait_ge(s_sq.h, 2 * c + 1)
                v.tensor_scalar(mcols[:, c:c + 1], sumc[:, c:c + 1], 1.0 / D, None, op0=ALU.mult)
                v.tensor_scalar(tv1[:], sumsqc[:, c:c + 1], 1.0 / D, None, op0=ALU.mult)
                v.drain()
                v.tensor_mul(tv2[:], mcols[:, c:c + 1], mcols[:, c:c + 1])
                v.drain()
                v.tensor_sub(tv1[:], tv1[:], tv2[:])
                v.drain()
                s_mx.inc(v.tensor_scalar(tv1[:], tv1[:], EPS, None, op0=ALU.add))
                v.wait_ge(s_sq.h, 2 * c + 2)
                v.reciprocal(rstdc[:, c:c + 1], tv2[:])
                v.drain()
                v.tensor_mul(tv1[:], mcols[:, c:c + 1], rstdc[:, c:c + 1])
                v.drain()
                v.tensor_scalar(nmrstd[:, c:c + 1], tv1[:], -1.0, None, op0=ALU.mult)
                v.drain()
                s_mx.inc(v.tensor_scalar(xa[par][:, 0:D], xa[par][:, 0:D], rstdc[:, c:c + 1],
                                         nmrstd[:, c:c + 1], op0=ALU.mult, op1=ALU.add))

        @block.scalar
        def _(s):
            for c in range(NT16):
                par = c % 2
                s.wait_ge(s_mx.h, 3 * c + 1)
                s_sq.inc(s.activation(sqs[:, :], xa[par][:, 0:D], ACT.Square, accum_out=sumsqc[:, c:c + 1]))
                s.wait_ge(s_mx.h, 3 * c + 2)
                s_sq.inc(s.activation(tv2[:], tv1[:], ACT.Sqrt))

        if CUT == 1:
            @block.gpsimd
            def _(g):
                g.wait_ge(s_x1s.h, X1S_DONE)
                s_o.inc(g.dma_start(out=out_d[:], in_=x1_d[:]), 16)
                g.wait_ge(s_o.h, 16)
            close_manual()
            return nc

        @block.gpsimd
        def _(g):
            for d in range(NDC):
                if d >= 1:
                    g.wait_ge(s_xr.h, d)
                s_xf.inc(g.dma_start(out=xfc[:, :],
                                     in_=x_fm_d.rearrange("(c p) t -> p c t", p=128)[:, d, :]), 16)

        @block.tensor
        def _(t):
            t.wait_ge(s_xg.h, XG_DONE)
            xr_ps = [pz[0][0:E, 0:512], pz[1][0:E, 0:512],
                     pz[2][0:E, 0:512], pz[3][0:E, 0:512]]
            for d in range(NDC):
                t.wait_ge(s_xf.h, 16 * (d + 1))
                for ti in range(NT4):
                    li = t.matmul(xr_ps[ti], rwf[:, d, :],
                                  xfc[:, 4 + 512 * ti:4 + 512 * ti + 512],
                                  start=(d == 0), stop=(d == NDC - 1))
                s_xr.inc(li)

        @block.scalar
        def _(s):
            s.wait_ge(s_xr.h, NDC)
            s.copy(xrw[:, 0:512], pz[0][0:E, 0:512])
            s.copy(xrw[:, 512:1024], pz[1][0:E, 0:512])
            s.copy(xrw[:, 1024:1536], pz[2][0:E, 0:512])
            s_xre.inc(s.copy(xrw[:, 1536:2048], pz[3][0:E, 0:512]))
        XRE_DONE = 1

        if CUT == 11:
            @block.gpsimd
            def _(g):
                g.wait_ge(s_xre.h, XRE_DONE)
                g.wait_ge(s_x1s.h, X1S_DONE)
                s_o.inc(g.dma_start(out=out_d[:], in_=x1_d[:]), 16)
                g.wait_ge(s_o.h, 16)
            close_manual()
            return nc

        RT = RTILES if CUT == 12 else NT16

        @block.tensor
        def _(t):
            t.wait_ge(s_xre.h, XRE_DONE)
            t.wait_ge(s_yc.h, 1)
            for c in range(RT):
                par = c % 2
                if c >= 2 and VSTEP >= 1:
                    t.wait_ge(s_lg.h, 2 * (c - 1))
                s_xrt.inc(t.matmul(ps_b[0:128, 8 * par:8 * par + 8],
                                   xrw[:, 128 * c:128 * (c + 1)], idf32[0:E, 0:E],
                                   start=True, stop=True))

        @block.vector
        def _(v):
            if VSTEP == 0:
                return
            for c in range(RT):
                par = c % 2
                v.wait_ge(s_xrt.h, c + 1)
                v.wait_ge(s_mx.h, 3 * c + 3)
                v.drain()
                v.tensor_scalar(negm[:, c:c + 1], mcols[:, c:c + 1], -1.0, None, op0=ALU.mult)
                v.scalar_tensor_tensor(t8a[:, :], wor_bc[:, :], ycols[:, c:c + 1],
                                       ps_b[0:128, 8 * par:8 * par + 8], op0=ALU.mult, op1=ALU.add)
                v.drain()
                v.scalar_tensor_tensor(t8b[:, :], srw_bc[:, :], negm[:, c:c + 1], t8a[:, :],
                                       op0=ALU.mult, op1=ALU.add)
                v.drain()
                v.tensor_scalar(lgt[:, :], t8b[:, :], rstdc[:, c:c + 1], None, op0=ALU.mult)
                v.drain()
                v.reduce_max(lmax[:], lgt[:, :], axis=AX)
                v.drain()
                s_lg.inc(v.tensor_scalar(nlmax[:], lmax[:], -1.0, None, op0=ALU.mult))
                if VSTEP == 1:
                    s_lg.bump(v)
                    continue
                v.wait_ge(s_ex.h, c + 1)
                v.reciprocal(gatec[:, c:c + 1], sexp[:])
                v.tensor_scalar(maskstm[:, c, :], lgt[:, :], lmax[:], None, op0=ALU.is_equal)
                v.tensor_scalar(mskbf[:, c, :], lgt[:, :], lmax[:], None, op0=ALU.is_equal)
                v.drain()
                v.tensor_mul(t8a[:, :], maskstm[:, c, :], iota8bc[:, :])
                v.drain()
                s_lg.inc(v.reduce_sum(ecols[:, c:c + 1], t8a[:, :], axis=AX))

        @block.scalar
        def _(s):
            if VSTEP == 0:
                return
            for c in range(RT):
                s.wait_ge(s_lg.h, 2 * c + 1)
                if VSTEP >= 2:
                    s_ex.inc(s.activation(expl[:, :], lgt[:, :], ACT.Exp, bias=nlmax[:], accum_out=sexp[:]))

        if CUT == 12:
            @block.gpsimd
            def _(g):
                if VSTEP == 0:
                    g.wait_ge(s_xrt.h, RT)
                else:
                    g.wait_ge(s_lg.h, 2 * RT)
                g.wait_ge(s_x1s.h, X1S_DONE)
                s_o.inc(g.dma_start(out=out_d[:], in_=x1_d[:]), 16)
                g.wait_ge(s_o.h, 16)
            close_manual()
            return nc

        if MQ != 2:
            @block.tensor
            def _(t):
                for c in range(NT16):
                    par = c % 2
                    t.wait_ge(s_lg.h, 2 * c + 2)
                    if c >= 1 and MQ != 1:
                        t.wait_ge(s_mke.h, c)
                    s_mk.inc(t.transpose(ptm[0:E, 128 * par:128 * par + 128], mskbf[:, c, :], idbf[:, :]))

        if MQ != 1:
            @block.vector
            def _(v):
                v.wait_ge(s_h.h, 1)
                v.memset(onesfm[:], 1.0)
                for c in range(NT16):
                    par = c % 2
                    if MQ != 2:
                        v.wait_ge(s_mk.h, c + 1)
                    s_mke.inc(v.tensor_copy(masksfm[:, 128 * c:128 * (c + 1)], ptm[0:E, 128 * par:128 * par + 128]))

        if CUT == 21:
            @block.gpsimd
            def _(g):
                g.wait_ge((s_mk if MQ == 1 else s_mke).h, NT16)
                g.wait_ge(s_x1s.h, X1S_DONE)
                s_o.inc(g.dma_start(out=out_d[:], in_=x1_d[:]), 16)
                g.wait_ge(s_o.h, 16)
            close_manual()
            return nc

        @block.vector
        def _(v):
            v.wait_ge(s_mke.h, NT16)
            v.drain()
            v.tensor_tensor_scan(cntfm[:, :], onesfm[:, :], masksfm[:, :], initial=0.0,
                                 op0=ALU.mult, op1=ALU.add)
            v.drain()
            v.tensor_copy(countsl[:], cntfm[:, TOK - 1:TOK])
            s_cn.inc(v.drain())

        if CUT == 22:
            @block.gpsimd
            def _(g):
                g.wait_ge(s_cn.h, 1)
                g.wait_ge(s_x1s.h, X1S_DONE)
                s_o.inc(g.dma_start(out=out_d[:], in_=x1_d[:]), 16)
                g.wait_ge(s_o.h, 16)
            close_manual()
            return nc

        @block.tensor
        def _(t):
            t.wait_ge(s_cn.h, 1)
            t.matmul(ps_b[0:E, 16:17], trilst[:, :], countsl[:], start=True, stop=True)
            s_cn.inc(t.drain())

        @block.vector
        def _(v):
            v.wait_ge(s_cn.h, 2)
            v.tensor_copy(gpref[:], ps_b[0:E, 16:17])
            s_cn.inc(v.drain())

        @block.tensor
        def _(t):
            t.wait_ge(s_cn.h, 3)
            t.transpose(ps_b[0:1, 24:32], gpref[:], idf32[0:E, 0:E])
            s_cn.inc(t.drain())

        @block.vector
        def _(v):
            v.wait_ge(s_cn.h, 4)
            v.tensor_copy(gprow[:], ps_b[0:1, 24:32])
            s_cn.inc(v.drain())

        @block.tensor
        def _(t):
            t.wait_ge(s_cn.h, 5)
            t.matmul(ps_b[0:128, 32:40], onesrow[:], gprow[:], start=True, stop=True)
            s_cn.inc(t.drain())

        @block.vector
        def _(v):
            v.wait_ge(s_cn.h, 6)
            v.tensor_copy(gprefT_bc[:], ps_b[0:128, 32:40])
            s_cn.inc(v.drain())
        CN_DONE = 7

        if CUT == 23:
            @block.gpsimd
            def _(g):
                g.wait_ge(s_cn.h, CN_DONE)
                g.wait_ge(s_x1s.h, X1S_DONE)
                s_o.inc(g.dma_start(out=out_d[:], in_=x1_d[:]), 16)
                g.wait_ge(s_o.h, 16)
            close_manual()
            return nc

        @block.tensor
        def _(t):
            t.wait_ge(s_cn.h, CN_DONE)
            for c in range(NT16):
                par = c % 2
                if c >= 2:
                    t.wait_ge(s_rk.h, c - 1)
                s_ct2.inc(t.transpose(ps_a[0:128, 8 * par:8 * par + 8],
                                      cntfm[:, 128 * c:128 * (c + 1)], idf32[0:E, 0:E]))

        @block.vector
        def _(v):
            for c in range(NT16):
                par = c % 2
                v.wait_ge(s_ct2.h, c + 1)
                v.tensor_mul(t8a[:, :], maskstm[:, c, :], ps_a[0:128, 8 * par:8 * par + 8])
                v.drain()
                v.reduce_sum(rlocc[:, c:c + 1], t8a[:, :], axis=AX)
                v.drain()
                v.tensor_mul(t8a[:, :], maskstm[:, c, :], gprefT_bc[:, :])
                v.drain()
                v.reduce_sum(tv1[:], t8a[:, :], axis=AX)
                v.drain()
                v.tensor_add(tv1[:], tv1[:], rlocc[:, c:c + 1])
                v.drain()
                s_rk.inc(v.tensor_scalar(gposc[:, c:c + 1], tv1[:], -1.0, None, op0=ALU.add))

        @block.vector
        def _(v):
            v.wait_ge(s_rk.h, NT16)
            v.drain()
            s_rk.inc(v.tensor_copy(gposi[:], gposc[:]))
        RK_DONE = NT16 + 1

        if CUT == 13:
            @block.gpsimd
            def _(g):
                g.wait_ge(s_rk.h, RK_DONE)
                g.wait_ge(s_x1s.h, X1S_DONE)
                s_o.inc(g.dma_start(out=out_d[:], in_=x1_d[:]), 16)
                g.wait_ge(s_o.h, 16)
            close_manual()
            return nc

        @block.tensor
        def _(t):
            t.wait_ge(s_cn.h, CN_DONE)
            t.transpose(ps_b[0:1, 40:48], countsl[:], idf32[0:E, 0:E])
            s_pr.inc(t.drain())

        @block.vector
        def _(v):
            v.wait_ge(s_pr.h, 1)
            v.tensor_copy(crow[:], ps_b[0:1, 40:48])
            s_pr.inc(v.drain())

        @block.sync
        def _(sy):
            sy.wait_ge(s_x1s.h, X1S_DONE)
            for c in range(NT16):
                par = c % 2
                if c >= 2:
                    sy.wait_ge(s_pkp[par].h, 16 * (c // 2))
                s_xlp[par].inc(sy.dma_start(out=xa[par][:, 0:D],
                                            in_=x1_d.rearrange("(c p) q -> p c q", p=128)[:, c, :]), 16)

        @block.vector
        def _(v):
            for c in range(NT16):
                par = c % 2
                v.wait_ge(s_xlp[par].h, 16 * (c // 2 + 1))
                v.wait_ge(s_lg.h, 2 * c + 2)
                s_meta.inc(v.tensor_copy(xa[par][:, D:D + 1], gatec[:, c:c + 1]))

        @block.gpsimd
        def _(g):
            g.wait_ge(s_rk.h, RK_DONE)
            for c in range(NT16):
                par = c % 2
                g.wait_ge(s_meta.h, c + 2)
                s_pkp[par].inc(g.indirect_dma_start(
                    out=pk_d[:], out_offset=IndirectOffsetOnAxis(ap=gposi[:, c:c + 1], axis=0),
                    in_=xa[par][:, 0:PW], in_offset=None), 16)
            g.wait_ge(s_pr.h, 2)
            g.wait_ge(s_g0.h, G0_DONE)
            s_pks.inc(g.dma_start(out=pk_d[CAP:CAP + 1, 0:E], in_=crow[:]), 16)
        PKS_DONE = s_pks.n

        DBG_EARLY = s_dbg.n

        if CUT == 15:
            @block.gpsimd
            def _(g):
                g.wait_ge(s_pks.h, PKS_DONE)
                g.wait_ge(s_pkp[0].h, s_pkp[0].n)
                g.wait_ge(s_pkp[1].h, s_pkp[1].n)
                s_o.inc(g.dma_start(out=out_d[:], in_=x1_d[:]), 16)
                g.wait_ge(s_o.h, 16)
            close_manual()
            return nc

        close_manual()
        ptm_c.__exit__(None, None, None)
        pproj_c.__exit__(None, None, None)
        for c_ in reversed(pz_c):
            c_[0].__exit__(None, None, None)
        ps_b_c.__exit__(None, None, None)
        ps_a_c.__exit__(None, None, None)

        w1b = sb([128, NDC, FF], BF16)
        w2b = sb([128, FF // 128, D], BF16)
        gf = [sb([128, PW]) for _ in range(2)]
        gb = [sb([128, D], BF16) for _ in range(2)]
        xe = sb([128, NDC, 256], BF16)
        hid = sb([128, FF // 128, 256], BF16)
        osb = [sb([128, D]) for _ in range(2)]

        pm1_c = [scoped_ps([128, 512], F32) for _ in range(2)]
        pm1 = [c[1] for c in pm1_c]
        ptr_c = [scoped_ps([128, 128], BF16) for _ in range(2)]
        ptr = [c[1] for c in ptr_c]
        pm2_c = [scoped_ps([128, 512], F32) for _ in range(4)]
        pm2 = [c[1] for c in pm2_c]

        @block.gpsimd
        def _(g):
            g.wait_ge(s_xr.h, NDC)
            if DEBUG:
                g.wait_ge(s_dbg.h, DBG_EARLY)
            s_wl.inc(g.dma_start(out=w1b[:], in_=w1_d.rearrange("(c p) f -> p c f", p=128)), 16)
            s_wl.inc(g.dma_start(out=w2b[:], in_=w2_d.rearrange("(c p) q -> p c q", p=128)), 16)
        WL_DONE = 32

        @block.gpsimd
        def _(g):
            g.wait_ge(s_pks.h, PKS_DONE)
            g.wait_ge(s_pkp[0].h, s_pkp[0].n)
            g.wait_ge(s_pkp[1].h, s_pkp[1].n)
            g.wait_ge(s_g0.h, G0_DONE)
            s_cc1.inc(g.collective_compute(
                "AllGather", ALU.bypass, ins=[pk_d.opt()], outs=[pall_d.opt()],
                replica_groups=RG), 1)
            g.wait_ge(s_cc1.h, 1)
            s_ag1.inc(g.dma_start(
                out=ctab[:],
                in_=pall_d.rearrange("(j r) q -> j r q", r=PROW)[:, CAP, 0:E]), 16)
        AG1_DONE = 16

        @block.vector
        def _(v):
            v.wait_ge(s_ag1.h, AG1_DONE)
            v.tensor_mul(t8a[0:E, :], ctab[:, :], onehotebc[:, :])
            v.drain()
            v.reduce_sum(countsE[:], t8a[0:E, :], axis=AX)
            v.drain()
            v.tensor_mul(t8a[0:E, :], ctab[:, :], ltmaskebc[:, :])
            v.drain()
            v.reduce_sum(goffE[:], t8a[0:E, :], axis=AX)
            s_idm.inc(v.drain())

        @block.tensor
        def _(t):
            t.wait_ge(s_idm.h, 1)
            t.matmul(pm1[0][0:E, 0:1], trilst[:, :], countsE[:], start=True, stop=True)
            t.matmul(pm1[0][0:E, 1:2], ctab[:, :], prevmask[:], start=True, stop=True)
            t.matmul(pm1[0][0:1, 2:3], ones128[0:E, 0:1], countsE[:], start=True, stop=True)
            s_idm.inc(t.drain())

        @block.vector
        def _(v):
            v.wait_ge(s_idm.h, 2)
            v.tensor_copy(coff[:], pm1[0][0:E, 0:1])
            v.tensor_add(cvals[:], j2056[:], goffE[:])
            v.drain()
            v.tensor_sub(cvals[:], cvals[:], coff[:])
            s_idm.inc(v.drain())

        @block.tensor
        def _(t):
            t.wait_ge(s_idm.h, 3)
            t.matmul(pm1[0][0:E, 3:4], diffm[:, :], cvals[:], start=True, stop=True)
            s_idm.inc(t.drain())

        @block.vector
        def _(v):
            v.wait_ge(s_idm.h, 4)
            v.tensor_copy(cpr[:], pm1[0][0:E, 3:4])
            s_idm.inc(v.drain())

        @block.tensor
        def _(t):
            t.wait_ge(s_idm.h, 5)
            t.transpose(pm1[1][0:1, 0:E], coff[:], idf32[0:E, 0:E])
            t.transpose(pm1[1][0:1, 8:8 + E], cpr[:], idf32[0:E, 0:E])
            s_idm.inc(t.drain())

        @block.vector
        def _(v):
            v.wait_ge(s_idm.h, 6)
            v.tensor_copy(crow[:], pm1[1][0:1, 0:E])
            v.tensor_copy(t8b[0:1, :], pm1[1][0:1, 8:8 + E])
            v.tensor_copy(tv1[0:1, :], pm1[0][0:1, 2:3])
            v.tensor_copy(gpref[:], pm1[0][0:E, 1:2])
            s_idm.inc(v.drain())

        @block.tensor
        def _(t):
            t.wait_ge(s_idm.h, 7)
            t.matmul(pm1[1][0:128, 0:8], onesrow[:], crow[:], start=True, stop=True)
            t.matmul(pm1[1][0:128, 8:16], onesrow[:], t8b[0:1, :], start=True, stop=True)
            t.matmul(pm1[1][0:128, 16:17], onesrow[:], tv1[0:1, 0:1], start=True, stop=True)
            t.transpose(pm1[1][0:1, 24:32], gpref[:], idf32[0:E, 0:E])
            s_idm.inc(t.drain())

        @block.vector
        def _(v):
            v.wait_ge(s_idm.h, 8)
            v.tensor_copy(coffb[:], pm1[1][0:128, 0:8])
            v.tensor_copy(cprb[:], pm1[1][0:128, 8:16])
            v.tensor_copy(countEb[:], pm1[1][0:128, 16:17])
            v.tensor_copy(crow[:], pm1[1][0:1, 24:32])
            s_idm.inc(v.drain())

        @block.tensor
        def _(t):
            t.wait_ge(s_idm.h, 9)
            t.matmul(pm1[1][0:128, 16:24], onesrow[:], crow[:], start=True, stop=True)
            s_idm.inc(t.drain())

        @block.vector
        def _(v):
            v.wait_ge(s_idm.h, 10)
            v.tensor_copy(offsT_bc[:], pm1[1][0:128, 16:24])
            v.tensor_copy(srcc[:], iotas[:])
            v.drain()
            for j in range(E):
                v.tensor_scalar(t16[:], iotas[:], coffb[:, j:j + 1], None, op0=ALU.is_ge)
                v.drain()
                v.scalar_tensor_tensor(srcc[:], t16[:], cprb[:, j:j + 1], srcc[:],
                                       op0=ALU.mult, op1=ALU.add)
                v.drain()
            v.tensor_scalar(srcc[:], srcc[:], float(E * PROW - 1), None, op0=ALU.min)
            v.drain()
            v.tensor_copy(srci[:], srcc[:])
            v.tensor_scalar(slotm[:], iotas[:], countEb[:, 0:1], None, op0=ALU.is_lt)
            for c in range(NT16):
                v.tensor_mul(t8a[:, :], maskstm[:, c, :], offsT_bc[:, :])
                v.drain()
                v.reduce_sum(tv1[:], t8a[:, :], axis=AX)
                v.drain()
                v.tensor_add(tv1[:], tv1[:], rlocc[:, c:c + 1])
                v.drain()
                v.tensor_scalar(grankc[:, c:c + 1], tv1[:], -1.0, None, op0=ALU.add)
                v.drain()
            v.tensor_scalar(yidxc[:], grankc[:], float(CAP), None, op0=ALU.min)
            v.tensor_scalar(t16[:], ecols[:], float(YROW), None, op0=ALU.mult)
            v.drain()
            v.tensor_add(yidxc[:], yidxc[:], t16[:])
            v.drain()
            v.tensor_copy(yidxi[:], yidxc[:])
            s_ide.inc(v.drain())
        IDE_DONE = 1

        if CUT == 2:
            @block.gpsimd
            def _(g):
                g.wait_ge(s_ide.h, IDE_DONE)
                s_o.inc(g.dma_start(out=out_d[:], in_=x1_d[:]), 16)
                g.wait_ge(s_o.h, 16)
            return nc

        @block.gpsimd
        def _(g):
            g.wait_ge(s_ide.h, IDE_DONE)
            for i in range(NT16):
                par = i % 2
                if i >= 2:
                    g.wait_ge(s_gf.h, i - 1)
                s_gtp[par].inc(g.indirect_dma_start(
                    out=gf[par][:, :], out_offset=None,
                    in_=pall_d[:], in_offset=IndirectOffsetOnAxis(ap=srci[:, i:i + 1], axis=0)), 16)


        @block.vector
        def _(v):
            for i in range(NT16):
                par = i % 2
                v.wait_ge(s_gtp[par].h, 16 * (i // 2 + 1))
                if i >= 2:
                    v.wait_ge(s_tr.h, 8 * (i - 1))
                v.tensor_mul(gateE[:, i:i + 1], gf[par][:, D:D + 1], slotm[:, i:i + 1])
                s_gf.inc(v.tensor_copy(gb[par][:, :], gf[par][:, 0:D]))
                v.wait_ge(s_m1.h, 32 * (i // 2))
                for d in range(NDC):
                    r = 8 * i + d
                    tp = r % 2
                    v.wait_ge(s_tr.h, r + 1)
                    s_te.inc(v.tensor_copy(xe[:, d, 128 * (i % 2):128 * (i % 2) + 128], ptr[tp][:, :]))

        @block.tensor
        def _(t):
            t.wait_ge(s_wl.h, WL_DONE)
            for ti in range(8):
                for i in (2 * ti, 2 * ti + 1):
                    par = i % 2
                    t.wait_ge(s_gf.h, i + 1)
                    for d in range(NDC):
                        r = 8 * i + d
                        tp = r % 2
                        if r >= 2:
                            t.wait_ge(s_te.h, r - 1)
                        s_tr.inc(t.transpose(ptr[tp][:, :], gb[par][:, 128 * d:128 * (d + 1)], idbf[:, :]))
                t.wait_ge(s_te.h, 16 * (ti + 1))
                for f in range(FF // 128):
                    m = 32 * ti + f
                    par = m % 2
                    if m >= 2:
                        t.wait_ge(s_ge.h, m - 1)
                    for d in range(NDC):
                        li = t.matmul(pm1[par][:, 0:256], w1b[:, d, 128 * f:128 * (f + 1)],
                                      xe[:, d, :], start=(d == 0), stop=(d == NDC - 1))
                    s_m1.inc(li)
                t.wait_ge(s_ge.h, 32 * (ti + 1))
                for t128 in range(2):
                    for half in range(2):
                        o = 4 * ti + 2 * t128 + half
                        pq = 2 * (t128 % 2) + half
                        if o >= 4:
                            t.wait_ge(s_oe.h, o - 3)
                        for f in range(FF // 128):
                            li = t.matmul(pm2[pq][:, :], hid[:, f, 128 * t128:128 * (t128 + 1)],
                                          w2b[:, f, 512 * half:512 * half + 512],
                                          start=(f == 0), stop=(f == FF // 128 - 1))
                        s_m2.inc(li)

        @block.scalar
        def _(s):
            for ti in range(8):
                for f in range(FF // 128):
                    m = 32 * ti + f
                    par = m % 2
                    if f == 0 and ti > 0:
                        s.wait_ge(s_m2.h, 4 * ti)
                    s.wait_ge(s_m1.h, m + 1)
                    s_ge.inc(s.activation(hid[:, f, :], pm1[par][:, 0:256], ACT.Gelu_apprx_tanh))
                for t128 in range(2):
                    for half in range(2):
                        o = 4 * ti + 2 * t128 + half
                        pq = 2 * (t128 % 2) + half
                        st_i = 2 * ti + t128
                        if st_i >= 2 and half == 0:
                            s.wait_ge(s_ysp[st_i % 2].h, 16 * (st_i // 2))
                        s.wait_ge(s_m2.h, o + 1)
                        s_oe.inc(s.activation(osb[t128 % 2][:, 512 * half:512 * half + 512], pm2[pq][:, :],
                                               ACT.Identity, scale=gateE[:, st_i:st_i + 1]))

        @block.sync
        def _(sy):
            sy.wait_ge(s_pr.h, 2)
            s_ys.inc(sy.dma_start(out=yk_d[CAP:CAP + 1, :], in_=zrow[:]), 16)
            for st_i in range(16):
                sy.wait_ge(s_oe.h, 2 * st_i + 2)
                s_ysp[st_i % 2].inc(sy.dma_start(
                    out=yk_d[128 * st_i:128 * (st_i + 1), :],
                    in_=osb[st_i % 2][:, :]), 16)
        YS_DONE = s_ys.n

        @block.gpsimd
        def _(g):
            g.wait_ge(s_ys.h, YS_DONE)
            g.wait_ge(s_ysp[0].h, s_ysp[0].n)
            g.wait_ge(s_ysp[1].h, s_ysp[1].n)
            s_cc2.inc(g.collective_compute(
                "AllGather", ALU.bypass, ins=[yk_d.opt()], outs=[yall_d.opt()],
                replica_groups=RG), 1)
        AG2_DONE = 1

        if CUT == 3:
            @block.gpsimd
            def _(g):
                g.wait_ge(s_cc2.h, AG2_DONE)
                s_o.inc(g.dma_start(out=out_d[:], in_=x1_d[:]), 16)
                g.wait_ge(s_o.h, 16)
            return nc

        @block.sync
        def _(sy):
            for c in range(2):
                s_x3p[c].inc(sy.dma_start(out=xa[c][:, 0:D],
                                          in_=x1_d.rearrange("(c p) q -> p c q", p=128)[:, c, :]), 16)
            for c in range(NT16):
                par = c % 2
                sy.wait_ge(s_l2.h, 3 * c + 3)
                s_o.inc(sy.dma_start(out=out_d.rearrange("(c p) q -> p c q", p=128)[:, c, :],
                                     in_=xa[par][:, 0:D]), 16)
                if c + 2 < NT16:
                    sy.wait_ge(s_o.h, 16 * (c + 1))
                    s_x3p[par].inc(sy.dma_start(out=xa[par][:, 0:D],
                                                in_=x1_d.rearrange("(c p) q -> p c q", p=128)[:, c + 2, :]), 16)
            sy.wait_ge(s_o.h, 16 * NT16)

        @block.gpsimd
        def _(g):
            g.wait_ge(s_cc2.h, AG2_DONE)
            for c in range(NT16):
                par = c % 2
                g.wait_ge(s_x3p[par].h, 16 * (c // 2 + 1))
                s_acp[par].inc(g.indirect_dma_start(
                    out=xa[par][:, 0:D], out_offset=None,
                    in_=yall_d[:], in_offset=IndirectOffsetOnAxis(ap=yidxi[:, c:c + 1], axis=0),
                    compute_op=ALU.add), 16)

        @block.vector
        def _(v):
            for c in range(NT16):
                par = c % 2
                v.wait_ge(s_acp[par].h, 16 * (c // 2 + 1))
                s_l2.inc(v.reduce_sum(sumc[:, c:c + 1], xa[par][:, 0:D], axis=AX))
                v.wait_ge(s_q2.h, 2 * c + 1)
                v.tensor_scalar(mcols[:, c:c + 1], sumc[:, c:c + 1], 1.0 / D, None, op0=ALU.mult)
                v.tensor_scalar(tv1[:], sumsqc[:, c:c + 1], 1.0 / D, None, op0=ALU.mult)
                v.drain()
                v.tensor_mul(tv2[:], mcols[:, c:c + 1], mcols[:, c:c + 1])
                v.drain()
                v.tensor_sub(tv1[:], tv1[:], tv2[:])
                v.drain()
                s_l2.inc(v.tensor_scalar(tv1[:], tv1[:], EPS, None, op0=ALU.add))
                v.wait_ge(s_q2.h, 2 * c + 2)
                v.reciprocal(rstdc[:, c:c + 1], tv2[:])
                v.drain()
                v.tensor_mul(tv1[:], mcols[:, c:c + 1], rstdc[:, c:c + 1])
                v.drain()
                v.tensor_scalar(nmrstd[:, c:c + 1], tv1[:], -1.0, None, op0=ALU.mult)
                v.drain()
                s_l2.inc(v.tensor_scalar(xa[par][:, 0:D], xa[par][:, 0:D], rstdc[:, c:c + 1],
                                         nmrstd[:, c:c + 1], op0=ALU.mult, op1=ALU.add))

        @block.scalar
        def _(s):
            for c in range(NT16):
                par = c % 2
                s.wait_ge(s_l2.h, 3 * c + 1)
                s_q2.inc(s.activation(sqs[:, :], xa[par][:, 0:D], ACT.Square, accum_out=sumsqc[:, c:c + 1]))
                s.wait_ge(s_l2.h, 3 * c + 2)
                s_q2.inc(s.activation(tv2[:], tv1[:], ACT.Sqrt))

        if DEBUG:
            @block.gpsimd
            def _(g):
                g.wait_ge(s_x1s.h, X1S_DONE)
                s_dbg.inc(g.dma_start(out=dbgx1_d[:], in_=x1_d[:]), 16)
                g.wait_ge(s_ys.h, YS_DONE)
                g.wait_ge(s_ysp[0].h, s_ysp[0].n)
                g.wait_ge(s_ysp[1].h, s_ysp[1].n)
                s_dbg.inc(g.dma_start(out=dbgyk_d[:], in_=yk_d[:]), 16)
                g.wait_ge(s_ide.h, IDE_DONE)
                for i, col in enumerate([ycols, sumc, rstdc, gatec, ecols,
                                         rlocc, gposc, grankc, yidxc, srcc]):
                    s_dbg.inc(g.dma_start(out=dbgcol_d[:, NT16 * i:NT16 * (i + 1)], in_=col[:]), 16)
                g.wait_ge(s_dbg.h, s_dbg.n)

    return nc


_NC_CACHE = {}


def _host_consts():
    c = {}
    c["idf32"] = np.eye(128, dtype=np.float32)
    c["loga"] = (-0.5 * np.log1p(np.arange(1, DS + 1, dtype=np.float32) / DS)).reshape(DS, 1)
    c["ones128"] = np.ones((128, 1), np.float32)
    c["onesrow"] = np.ones((1, 128), np.float32)
    c["iota8bc"] = np.tile(np.arange(E, dtype=np.float32), (128, 1))
    c["iotas"] = (np.arange(NT16, dtype=np.float32)[None, :] * 128
                  + np.arange(128, dtype=np.float32)[:, None]).astype(np.float32)
    c["trilst"] = np.triu(np.ones((E, E), np.float32), 1)
    c["j2056"] = (np.arange(E, dtype=np.float32) * PROW).reshape(E, 1)
    dm = np.zeros((E, E), np.float32)
    for m in range(E):
        dm[m, m] = 1.0
        if m + 1 < E:
            dm[m, m + 1] = -1.0
    c["diffm"] = dm
    return c


def kernel(**inputs):
    import ml_dtypes

    x = np.ascontiguousarray(np.asarray(inputs["L0_attn_x"], np.float32))
    w_in = np.ascontiguousarray(np.asarray(inputs["L0_attn_W_in"], np.float32))
    def chunkmajor(a, p=128):
        a = np.asarray(a, np.float32)
        c = a.shape[0] // p
        return np.ascontiguousarray(a.reshape(c, p, -1).transpose(1, 0, 2).reshape(p, -1))

    convw = chunkmajor(np.asarray(inputs["L0_attn_conv_w"], np.float32).reshape(DI, DC))
    convb = chunkmajor(np.asarray(inputs["L0_attn_conv_b"], np.float32).reshape(DI, 1))
    wdelta = chunkmajor(inputs["L0_attn_W_delta"])
    bdelta = np.asarray(inputs["L0_attn_b_delta"], np.float32).reshape(DS, 1)
    wB = chunkmajor(inputs["L0_attn_W_B"])
    wC = chunkmajor(inputs["L0_attn_W_C"])
    dvec = chunkmajor(np.asarray(inputs["L0_attn_D"], np.float32).reshape(DI, 1))
    wout_cm = chunkmajor(np.asarray(inputs["L0_attn_w_out"], np.float32).reshape(D, 1))
    wout = np.asarray(inputs["L0_attn_w_out"], np.float32).reshape(D, 1)
    rw = chunkmajor(np.asarray(inputs["L0_ffn_router_w"], np.float32))
    w1 = np.asarray(inputs["L0_ffn_w1"], np.float32)
    w2 = np.asarray(inputs["L0_ffn_w2"], np.float32)

    consts = _host_consts()
    idbf = np.eye(128).astype(ml_dtypes.bfloat16)
    woutbc = np.tile(wout.reshape(1, D), (128, 1)).astype(np.float32)

    in_maps = []
    for k in range(8):
        b, j = k // 4, k % 4
        t0, t1 = j * TOK, (j + 1) * TOK
        xs = x[b, t0:t1, :]
        xfm_ext = np.zeros((D, XC), np.float32)
        lo = max(t0 - 4, 0)
        if t0 > 0:
            xfm_ext[:, 4 - (t0 - lo):4] = x[b, lo:t0, :].T
        xfm_ext[:, 4:] = xs.T
        foldsel = np.zeros((E, 4), np.float32)
        fixa = np.zeros((DS, 4), np.float32)
        for cstep in range(4):
            if cstep < j:
                foldsel[4 * b + cstep, cstep] = 1.0
            else:
                fixa[:, cstep] = 1.0
        prevmask = np.zeros((E, 1), np.float32)
        prevmask[:k, 0] = 1.0
        onehotebc = np.zeros((E, E), np.float32)
        onehotebc[:, k] = 1.0
        ltmaskebc = np.zeros((E, E), np.float32)
        ltmaskebc[:, :k] = 1.0
        im = {
            "x_tm": np.ascontiguousarray(xs),
            "x_fm": np.ascontiguousarray(xfm_ext),
            "w_in": w_in, "convw": convw, "convb": convb,
            "wdel": wdelta, "bdel": bdelta, "wB": wB, "wC": wC,
            "dvec": dvec, "wout": wout_cm, "woutbc": woutbc, "rw": rw,
            "w1": np.ascontiguousarray(w1[k]), "w2": np.ascontiguousarray(w2[k]),
            "idbf": idbf,
            "foldsel": foldsel, "fixa": fixa, "prevmask": prevmask,
            "onehotebc": onehotebc, "ltmaskebc": ltmaskebc,
        }
        im.update(consts)
        in_maps.append(im)

    key = ("v1", DEBUG, CUT)
    if key not in _NC_CACHE:
        _NC_CACHE[key] = build_nc()
    nc = _NC_CACHE[key]
    res = run_bass_kernel_spmd(nc, in_maps, core_ids=list(range(8)))
    out = np.empty((B, L, D), np.float32)
    for k in range(8):
        b, j = k // 4, k % 4
        out[b, j * TOK:(j + 1) * TOK, :] = res.results[k]["out"]
    if DEBUG:
        kernel._dbg = res.results
    return out
